# version 01
# baseline (speedup 1.0000x reference)
import os
import numpy as np
from contextlib import ExitStack

import concourse.bass as bass
import concourse.mybir as mybir
from concourse.bass_utils import run_bass_kernel_spmd

F32 = mybir.dt.float32
BF16 = mybir.dt.bfloat16
I32 = mybir.dt.int32
A = mybir.AluOpType
AF = mybir.ActivationFunctionType
AX = mybir.AxisListType.X

D = 1024
CH = 2048
PRE = 6144
TS = 256
PTS = 512
NSUB = CH // TS
NPRE = PRE // PTS
NCH = D // 128
LN2 = float(np.log(2.0))
INVLN2 = float(1.0 / np.log(2.0))
LNEPS = 1e-5


def rep32(ap2):
    return bass.AP(ap2.tensor, ap2.offset, [ap2.ap[0], ap2.ap[1], [0, 32]])


def red3(ap2):
    return bass.AP(ap2.tensor, ap2.offset, [ap2.ap[0], ap2.ap[1], [0, 1]])


def blk32(ap2):
    return ap2.rearrange("p (b k) -> p b k", k=32)


def bc8(ap2, T):
    return bass.AP(ap2.tensor, ap2.offset, [ap2.ap[0], [0, NCH], [1, T]])


def build_nc():
    nc = bass.Bass("TRN2", target_bir_lowering=False)
    dt = nc.dram_tensor
    xm = dt("xm", [CH, D], F32, kind="ExternalInput").ap()
    xp = dt("xp", [PRE, D], F32, kind="ExternalInput").ap()
    pspre = dt("pspre", [128, PRE], F32, kind="ExternalInput").ap()
    psmain = dt("psmain", [128, CH], F32, kind="ExternalInput").ap()
    ipmain = dt("ipmain", [128, CH], F32, kind="ExternalInput").ap()
    wom = dt("wom", [D, D], F32, kind="ExternalInput").ap()
    bomf = dt("bomf", [128, NCH], F32, kind="ExternalInput").ap()
    lscf = dt("lscf", [128, NCH], F32, kind="ExternalInput").ap()
    g2f = dt("g2f", [128, 32], F32, kind="ExternalInput").ap()
    b2gf = dt("b2gf", [128, 32], F32, kind="ExternalInput").ap()
    w1 = dt("w1", [2 * D, 4 * D], F32, kind="ExternalInput").ap()
    b1r = dt("b1r", [1, 2 * D], F32, kind="ExternalInput").ap()
    w2 = dt("w2", [D, 2 * D], F32, kind="ExternalInput").ap()
    b2r = dt("b2r", [1, D], F32, kind="ExternalInput").ap()
    wfc = dt("wfc", [4 * D, D], F32, kind="ExternalInput").ap()
    bfcr = dt("bfcr", [1, 4 * D], F32, kind="ExternalInput").ap()
    wpj = dt("wpj", [D, 4 * D], F32, kind="ExternalInput").ap()
    bpjr = dt("bpjr", [1, D], F32, kind="ExternalInput").ap()
    y = dt("y", [CH, D], F32, kind="ExternalOutput").ap()
    DBG = os.environ.get("KDBG", "") == "1"
    KSER = os.environ.get("KSER", "") == "1"
    if DBG:
        dbg_xn = dt("dbg_xn", [128, 2048], F32, kind="ExternalOutput").ap()
        dbg_phi = dt("dbg_phi", [128, 2048], F32, kind="ExternalOutput").ap()
        dbg_cos = dt("dbg_cos", [128, 2048], F32, kind="ExternalOutput").ap()
        dbg_ctx = dt("dbg_ctx", [128, 16384], F32, kind="ExternalOutput").ap()
        dbg_s = dt("dbg_s", [128, 4096], F32, kind="ExternalOutput").ap()
        dbg_g = dt("dbg_g", [128, 2048], F32, kind="ExternalOutput").ap()
        dbg_g2 = dt("dbg_g2", [128, 2048], F32, kind="ExternalOutput").ap()
        dbg_s2 = dt("dbg_s2", [128, 2048], F32, kind="ExternalOutput").ap()
        dbg_wfcq = dt("dbg_wfcq", [D, 4 * D], BF16, kind="ExternalOutput").ap()
        dbg_wpjq = dt("dbg_wpjq", [4 * D, D], BF16, kind="ExternalOutput").ap()
        dbg_xq = dt("dbg_xq", [128, 2048], BF16, kind="ExternalOutput").ap()
        dbg_hq = dt("dbg_hq", [128, 4096], BF16, kind="ExternalOutput").ap()
        dbg_car = dt("dbg_car", [128, 384], F32, kind="ExternalOutput").ap()
        dbg_h1 = dt("dbg_h1", [128, 4096], F32, kind="ExternalOutput").ap()
    womh_s = dt("womh_s", [D, D], BF16).ap()
    woml_s = dt("woml_s", [D, D], BF16).ap()
    w1h_s = dt("w1h_s", [4 * D, 2 * D], BF16).ap()
    w1l_s = dt("w1l_s", [4 * D, 2 * D], BF16).ap()
    w2h_s = dt("w2h_s", [2 * D, D], BF16).ap()
    w2l_s = dt("w2l_s", [2 * D, D], BF16).ap()
    wfcq_s = dt("wfcq_s", [D, 4 * D], BF16).ap()
    wpjq_s = dt("wpjq_s", [4 * D, D], BF16).ap()

    with ExitStack() as es:
        E = es.enter_context
        MA = E(nc.sbuf_tensor("MA", [128, 16384], F32))
        XNS = E(nc.sbuf_tensor("XNS", [128, 4096], F32))
        XT = E(nc.sbuf_tensor("XT", [128, 1024], F32))
        SQD = E(nc.sbuf_tensor("SQD", [128, 1024], F32))
        ROWS = E(nc.sbuf_tensor("ROWS", [1, 8448], F32))
        PSI = E(nc.sbuf_tensor("PSI", [128, 1024], F32))
        STT = E(nc.sbuf_tensor("STT", [128, 1152], F32))
        SCL = E(nc.sbuf_tensor("SCL", [128, 640], F32))
        CAR = E(nc.sbuf_tensor("CAR", [128, 48], F32))
        CON = E(nc.sbuf_tensor("CON", [128, 16], F32))
        VEC = E(nc.sbuf_tensor("VEC", [128, 32], F32))
        G2C = E(nc.sbuf_tensor("G2C", [128, 64], F32))
        WS = E(nc.sbuf_tensor("WS", [128, 4096], BF16))
        XSP = E(nc.sbuf_tensor("XSP", [128, 8192], BF16))
        CTX = E(nc.sbuf_tensor("CTX", [128, 16896], BF16))
        OB = E(nc.sbuf_tensor("OB", [128, 128], BF16))
        IA = E(nc.sbuf_tensor("IA", [128, 4096], I32))
        psA = E(nc.psum_tensor("psA", [128, 2048], F32))
        psB = E(nc.psum_tensor("psB", [128, 2048], F32))

        sems = {n: E(nc.semaphore(n)) for n in
                ["qd", "gd", "vv", "ss", "tt", "wc"]}
        cnt = {n: 0 for n in sems}
        block = E(nc.Block())

        def inc(inst, name, by):
            cnt[name] += by
            inst.then_inc(sems[name], by)

        def wat(eng, name, val=None):
            eng.wait_ge(sems[name], cnt[name] if val is None else val)

        def barrier():
            if not KSER:
                return
            vals = dict(cnt)

            def mk(names):
                def f(e):
                    for n in names:
                        e.wait_ge(sems[n], vals[n])
                return f
            block.vector(mk(["qd", "gd", "ss", "tt"]))
            block.scalar(mk(["qd", "gd", "vv", "tt"]))
            block.tensor(mk(["qd", "gd", "vv", "ss"]))
            block.sync(mk(["gd", "vv", "ss", "tt"]))
            block.gpsimd(mk(["qd", "vv", "ss", "tt"]))

        def vf(v, inst):
            inc(inst, "vv", 1)
            v.wait_ge(sems["vv"], cnt["vv"])

        V_EXPS, V_BEFF, V_BOM, V_LSC = 0, 8, 16, 24
        C165, C05, CEPS, CLN2, CNLN2, CBS, CBI = 0, 1, 2, 3, 4, 5, 6
        R_ONES, R_B1, R_B2, R_BFC, R_BPJ = 0, 128, 2176, 3200, 7296
        CPHI, CR, CI, ACR, ACI = 0, 8, 16, 24, 32

        @block.sync
        def _(q):
            inc(q.dma_start(out=VEC[:, V_LSC:V_LSC + 8], in_=lscf[:]), "qd", 16)
            inc(q.dma_start(out=VEC[:, V_BOM:V_BOM + 8], in_=bomf[:]), "qd", 16)
            inc(q.dma_start(out=G2C[:, 0:32], in_=g2f[:]), "qd", 16)
            inc(q.dma_start(out=G2C[:, 32:64], in_=b2gf[:]), "qd", 16)
            inc(q.dma_start(out=ROWS[0:1, R_B1:R_B1 + 2048], in_=b1r[:]), "qd", 16)
            inc(q.dma_start(out=ROWS[0:1, R_B2:R_B2 + 1024], in_=b2r[:]), "qd", 16)
            inc(q.dma_start(out=ROWS[0:1, R_BFC:R_BFC + 4096], in_=bfcr[:]),
                "qd", 16)
            inc(q.dma_start(out=ROWS[0:1, R_BPJ:R_BPJ + 1024], in_=bpjr[:]),
                "qd", 16)
        ld0 = cnt["qd"]

        @block.vector
        def _(v):
            v.memset(CON[:, C165:C165 + 1], 16.5)
            v.memset(CON[:, C05:C05 + 1], 0.5)
            v.memset(CON[:, CEPS:CEPS + 1], LNEPS)
            v.memset(CON[:, CLN2:CLN2 + 1], LN2)
            v.memset(CON[:, CNLN2:CNLN2 + 1], -LN2)
            v.memset(CON[:, CBS:CBS + 1], -3.0 * LN2)
            v.memset(CON[:, CBI:CBI + 1], 3.0 * LN2)
            v.memset(OB[:], 1.0)
            v.memset(ROWS[0:1, R_ONES:R_ONES + 128], 1.0)
            inc(v.memset(CAR[:], 0.0), "vv", 1)

        @block.scalar
        def _(s):
            wat(s, "qd", ld0)
            inc(s.activation(VEC[:, V_EXPS:V_EXPS + 8], VEC[:, V_LSC:V_LSC + 8],
                             AF.Exp, bias=0.0, scale=1.0), "ss", 1)

        @block.vector
        def _(v):
            wat(v, "ss")
            inc(v.tensor_tensor(VEC[:, V_BEFF:V_BEFF + 8], VEC[:, V_BOM:V_BOM + 8],
                                VEC[:, V_EXPS:V_EXPS + 8], A.mult), "vv", 1)

        pW = MA[:, 0:4096]
        pT1 = MA[:, 4096:8192]
        pT2 = MA[:, 8192:12288]
        pT3 = MA[:, 12288:16384]
        cH = CTX[:, 0:4096]
        cL = CTX[:, 4096:8192]
        cS = CTX[:, 8192:12288]
        cS2 = CTX[:, 12288:16384]

        def scratch_write_ap(dst, row0, col0, nkc):
            C = dst.ap[0][0]
            return bass.AP(dst.tensor, dst.offset + row0 * C + col0,
                           [[C, 128], [128 * C, nkc], [1, 128]])

        def prep_split(src, n_rows, K, h_s, l_s):
            nkc = K // 128
            for r in range(n_rows // 128):
                gd_w = cnt["gd"]
                qd_tile = cnt["qd"] + 16

                @block.sync
                def _(q):
                    wat(q, "gd", gd_w)
                    inc(q.dma_start(out=pW[:, 0:K],
                                    in_=src[r * 128:(r + 1) * 128, :]), "qd", 16)

                @block.vector
                def _(v):
                    wat(v, "qd", qd_tile)
                    v.tensor_copy(cH[:, 0:K], pW[:, 0:K])
                    inc(v.scalar_tensor_tensor(cL[:, 0:K], cH[:, 0:K], -1.0,
                                               pW[:, 0:K], A.mult, A.add), "vv", 1)
                vv_spl = cnt["vv"]

                @block.sync
                def _(q):
                    wat(q, "vv", vv_spl)
                    for csrc, cdst in [(cH, cS), (cL, cS2)]:
                        for kc in range(nkc):
                            inc(q.dma_start(out=cdst[:, kc * 128:(kc + 1) * 128],
                                            in_=csrc[:, kc * 128:(kc + 1) * 128],
                                            transpose=True), "qd", 16)
                qd_tp = cnt["qd"]

                @block.gpsimd
                def _(g):
                    wat(g, "qd", qd_tp)
                    for cdst, dst_s in [(cS, h_s), (cS2, l_s)]:
                        inc(g.dma_start(
                            out=scratch_write_ap(dst_s, 0, r * 128, nkc),
                            in_=cdst[:, 0:K].rearrange("p (c q) -> p c q", q=128)),
                            "gd", 16)

        def emit_quant(vt, qout, Kf, t1, t2, t3, tie):
            nb = Kf // 32
            am = SCL[:, 0:nb]
            sc_s = SCL[:, 128:128 + nb]
            sc_i = SCL[:, 256:256 + nb]
            sc_t = SCL[:, 384:384 + nb]
            ie = IA[:, 3968:3968 + nb]
            i0 = IA[:, 0:Kf]

            @block.vector
            def _(v):
                wat(v, "ss")
                vf(v, v.tensor_reduce(red3(am), blk32(vt), AX, A.max,
                                      apply_absolute_value=True))
                inc(v.tensor_scalar(sc_t, am, float(2.0 ** -30), None, A.max),
                    "vv", 1)
            c1 = cnt["vv"]

            @block.scalar
            def _(s):
                wat(s, "vv", c1)
                inc(s.activation(sc_s, sc_t, AF.Ln, bias=0.0, scale=1.0), "ss", 1)
            c2 = cnt["ss"]

            @block.vector
            def _(v):
                wat(v, "ss", c2)
                inc(v.tensor_scalar(sc_t, sc_s, INVLN2, 1e-9, A.mult, A.add),
                    "vv", 1)
            c3 = cnt["vv"]

            @block.scalar
            def _(s):
                wat(s, "vv", c3)
                inc(s.activation(ie, sc_t, AF.Identity, bias=0.0, scale=1.0),
                    "ss", 1)
            c4 = cnt["ss"]

            @block.vector
            def _(v):
                wat(v, "ss", c4)
                vf(v, v.scalar_tensor_tensor(sc_i, ie, -1.0, sc_t, A.mult, A.add))
                vf(v, v.tensor_scalar(sc_i, sc_i, 0.0, None, A.is_lt))
                inc(v.scalar_tensor_tensor(sc_t, sc_i, -1.0, ie, A.mult, A.add),
                    "vv", 1)
            c5 = cnt["vv"]

            @block.scalar
            def _(s):
                wat(s, "vv", c5)
                s.activation(sc_s, sc_t, AF.Exp, bias=CON[:, CBS:CBS + 1],
                             scale=CON[:, CLN2:CLN2 + 1])
                inc(s.activation(sc_i, sc_t, AF.Exp, bias=CON[:, CBI:CBI + 1],
                                 scale=CON[:, CNLN2:CNLN2 + 1]), "ss", 1)
            c6 = cnt["ss"]

            @block.vector
            def _(v):
                wat(v, "ss", c6)
                inc(v.tensor_tensor(blk32(t1), blk32(vt), rep32(sc_i), A.mult),
                    "vv", 1)
            c7 = cnt["vv"]

            @block.scalar
            def _(s):
                wat(s, "vv", c7)
                inc(s.activation(i0, t1, AF.Identity,
                                 bias=CON[:, C165:C165 + 1], scale=1.0), "ss", 1)
            c8 = cnt["ss"]

            if not tie:
                @block.vector
                def _(v):
                    wat(v, "ss", c8)
                    v.scalar_tensor_tensor(t2, i0, -1.0, t1, A.mult, A.add)
                    v.tensor_scalar(t1, t2, -15.5, None, A.is_gt)
                    v.tensor_scalar(t3, t2, -16.5, None, A.is_lt)
                    v.tensor_tensor(t1, t1, t3, A.subtract)
                    v.tensor_tensor(t1, t1, i0, A.add)
                    v.tensor_scalar(t1, t1, -16.0, None, A.add)
                    inc(v.tensor_tensor(blk32(qout), blk32(t1), rep32(sc_s),
                                        A.mult), "vv", 1)
            else:
                assert Kf <= 2048
                i2 = IA[:, 2048:2048 + Kf]

                @block.vector
                def _(v):
                    wat(v, "ss", c8)
                    v.scalar_tensor_tensor(t2, i0, -1.0, t1, A.mult, A.add)
                    v.tensor_scalar(t1, t2, -15.5, None, A.is_gt)
                    v.tensor_scalar(t3, t2, -16.5, None, A.is_lt)
                    v.tensor_tensor(t1, t1, t3, A.subtract)
                    v.tensor_scalar(t3, i0, -16.0, 0.5, A.add, A.mult)
                    inc(v.tensor_scalar(t3, t3, 16.5, None, A.add), "vv", 1)
                c9 = cnt["vv"]

                @block.scalar
                def _(s):
                    wat(s, "vv", c9)
                    inc(s.activation(i2, t3, AF.Identity, bias=0.0, scale=1.0),
                        "ss", 1)
                c10 = cnt["ss"]

                @block.vector
                def _(v):
                    wat(v, "ss", c10)
                    v.scalar_tensor_tensor(t3, i2, -1.0, t3, A.mult, A.add)
                    v.tensor_scalar(t3, t3, 0.0, None, A.is_equal)
                    v.tensor_scalar(vt, t2, -15.5, None, A.is_equal)
                    v.tensor_scalar(t2, t2, -16.5, None, A.is_equal)
                    v.tensor_tensor(vt, vt, t2, A.subtract)
                    v.tensor_tensor(vt, vt, t3, A.mult)
                    v.tensor_tensor(t1, t1, vt, A.add)
                    v.tensor_tensor(t1, t1, i0, A.add)
                    v.tensor_scalar(t1, t1, -16.0, None, A.add)
                    inc(v.tensor_tensor(blk32(qout), blk32(t1), rep32(sc_s),
                                        A.mult), "vv", 1)

        def prep_quant(src, n_rows, K, q_s):
            half = min(K, 2048)
            for r in range(n_rows // 128):
                for h0 in range(0, K, half):
                    Kf = half
                    gd_w = cnt["gd"]
                    qd_tile = cnt["qd"] + 16

                    @block.sync
                    def _(q):
                        wat(q, "gd", gd_w)
                        inc(q.dma_start(out=pW[:, 0:Kf],
                                        in_=src[r * 128:(r + 1) * 128,
                                                h0:h0 + Kf]), "qd", 16)

                    @block.vector
                    def _(v):
                        wat(v, "qd", qd_tile)
                        inc(v.tensor_copy(SCL[:, 512:513], CON[:, 0:1]), "vv", 1)

                    barrier()
                    emit_quant(pW[:, 0:Kf], cH[:, 0:Kf], Kf, pT1[:, 0:Kf],
                               pT2[:, 0:Kf], pT3[:, 0:Kf], tie=True)
                    vv_q = cnt["vv"]

                    @block.sync
                    def _(q):
                        wat(q, "vv", vv_q)
                        for kc in range(Kf // 128):
                            inc(q.dma_start(out=cS[:, kc * 128:(kc + 1) * 128],
                                            in_=cH[:, kc * 128:(kc + 1) * 128],
                                            transpose=True), "qd", 16)
                    qd_tp = cnt["qd"]

                    @block.gpsimd
                    def _(g):
                        wat(g, "qd", qd_tp)
                        inc(g.dma_start(
                            out=scratch_write_ap(q_s, h0, r * 128, Kf // 128),
                            in_=cS[:, 0:Kf].rearrange("p (c q) -> p c q", q=128)),
                            "gd", 16)

        prep_split(wom, D, D, womh_s, woml_s)
        prep_split(w1, 2 * D, 4 * D, w1h_s, w1l_s)
        prep_split(w2, D, 2 * D, w2h_s, w2l_s)
        prep_quant(wfc, 4 * D, D, wfcq_s)
        prep_quant(wpj, D, 4 * D, wpjq_s)

        def ln1_tile(xsrc_hbm, toff, xdst_tm, hdst, ldst):
            qd_t = cnt["qd"] + 16

            @block.sync
            def _(q):
                wat(q, "vv")
                inc(q.dma_start(out=XT[:], in_=xsrc_hbm[toff:toff + 128, :]),
                    "qd", 16)

            @block.scalar
            def _(s):
                wat(s, "qd", qd_t)
                s.activation(SQD[:], XT[:], AF.Square, bias=0.0, scale=1.0,
                             accum_out=STT[:, 1025:1026])
                inc(s.activation(SQD[:], XT[:], AF.Identity, bias=0.0, scale=1.0,
                                 accum_out=STT[:, 1024:1025]), "ss", 1)
            ss_st = cnt["ss"]

            @block.vector
            def _(v):
                wat(v, "ss", ss_st)
                v.tensor_scalar(STT[:, 1036:1037], STT[:, 1024:1025],
                                1.0 / 1024.0, None, A.mult)
                vf(v, v.tensor_scalar(STT[:, 1037:1038], STT[:, 1025:1026],
                                      1.0 / 1024.0, None, A.mult))
                vf(v, v.tensor_tensor(STT[:, 1028:1029], STT[:, 1036:1037],
                                      STT[:, 1036:1037], A.mult))
                inc(v.tensor_tensor(STT[:, 1037:1038], STT[:, 1037:1038],
                                    STT[:, 1028:1029], A.subtract), "vv", 1)
            vv_st = cnt["vv"]

            @block.scalar
            def _(s):
                wat(s, "vv", vv_st)
                inc(s.activation(STT[:, 1038:1039], STT[:, 1037:1038], AF.Sqrt,
                                 bias=CON[:, CEPS:CEPS + 1], scale=1.0), "ss", 1)
            ss_r = cnt["ss"]

            @block.vector
            def _(v):
                wat(v, "ss", ss_r)
                vf(v, v.reciprocal(STT[:, 1038:1039], STT[:, 1038:1039]))
                v.tensor_scalar(xdst_tm, XT[:], STT[:, 1036:1037],
                                STT[:, 1038:1039], A.subtract, A.mult)
                v.tensor_copy(hdst, xdst_tm)
                inc(v.scalar_tensor_tensor(ldst, hdst, -1.0, xdst_tm,
                                           A.mult, A.add), "vv", 1)

        def omega_phase(xhf, xlf, T, ps_list, psm_sl, dst):
            wcb = cnt["wc"]
            for kc in range(NCH):
                slot = kc % 2
                hbuf = WS[:, slot * 2048:slot * 2048 + 1024]
                lbuf = WS[:, slot * 2048 + 1024:slot * 2048 + 2048]

                @block.sync
                def _(q):
                    if kc == 0:
                        wat(q, "tt")
                    if kc >= 2:
                        q.wait_ge(sems["wc"], wcb + kc - 1)
                    inc(q.dma_start(out=hbuf,
                                    in_=womh_s[kc * 128:(kc + 1) * 128, :]),
                        "qd", 16)
                    inc(q.dma_start(out=lbuf,
                                    in_=woml_s[kc * 128:(kc + 1) * 128, :]),
                        "qd", 16)
                qd_w = cnt["qd"]

                @block.tensor
                def _(t):
                    wat(t, "qd", qd_w)
                    rh = xhf[:, kc * T:(kc + 1) * T]
                    rl = xlf[:, kc * T:(kc + 1) * T]
                    last = None
                    for c in range(NCH):
                        wh = hbuf[:, c * 128:(c + 1) * 128]
                        wl = lbuf[:, c * 128:(c + 1) * 128]
                        t.matmul(ps_list[c], wh, rh, start=(kc == 0), stop=False)
                        t.matmul(ps_list[c], wh, rl, start=False, stop=False)
                        last = t.matmul(ps_list[c], wl, rh, start=False,
                                        stop=(kc == NCH - 1))
                    if kc == NCH - 1:
                        inc(last, "tt", 1)
                    else:
                        inc(last, "wc", 1)
            tt_mm = cnt["tt"]

            @block.scalar
            def _(s):
                wat(s, "tt", tt_mm)
                wat(s, "vv")
                last = None
                for c in range(NCH):
                    last = s.activation(dst[:, c * T:(c + 1) * T], ps_list[c],
                                        AF.Identity,
                                        bias=VEC[:, V_BEFF + c:V_BEFF + c + 1],
                                        scale=VEC[:, V_EXPS + c:V_EXPS + c + 1])
                inc(last, "ss", 1)
            ss_ev = cnt["ss"]

            @block.vector
            def _(v):
                wat(v, "ss", ss_ev)
                last = None
                for c in range(NCH):
                    sl = dst[:, c * T:(c + 1) * T]
                    last = v.tensor_tensor(sl, sl, psm_sl, A.mult)
                inc(last, "vv", 1)

        def scan_phase(src, dst, T, car_col):
            @block.vector
            def _(v):
                last = None
                for c in range(NCH):
                    last = v.tensor_tensor_scan(
                        dst[:, c * T:(c + 1) * T], src[:, c * T:(c + 1) * T],
                        src[:, c * T:(c + 1) * T],
                        CAR[:, car_col + c:car_col + c + 1], A.add, A.bypass)
                vf(v, last)
                lastc = bass.AP(dst.tensor, dst.offset + T - 1,
                                [dst.ap[0], [T, NCH]])
                vf(v, v.tensor_copy(CAR[:, car_col:car_col + NCH], lastc))
                inc(v.tensor_copy(SCL[:, 519:520], CON[:, 0:1]), "vv", 1)

        def sincos_phase(phi, sin_dst, cos_dst):
            vv_in = cnt["vv"]

            @block.scalar
            def _(s):
                wat(s, "vv", vv_in)
                s.activation(sin_dst, phi, AF.Sin, bias=0.0, scale=1.0)
                s.activation(cos_dst, phi, AF.Sin, bias=0.0,
                             scale=CON[:, C05:C05 + 1])
                inc(s.activation(cos_dst, cos_dst, AF.Square, bias=0.0,
                                 scale=1.0), "ss", 1)
            ss_sc = cnt["ss"]

            @block.vector
            def _(v):
                wat(v, "ss", ss_sc)
                inc(v.tensor_scalar(cos_dst, cos_dst, -2.0, 1.0, A.mult, A.add),
                    "vv", 1)

        def stream_mm(w_srcs, nkc, NOUT, stat_fn, ps_of, bias_row, evict_fn,
                      npass3, ev_guard="ss"):
            nslice = NOUT // 512
            wcb = cnt["wc"]
            idx = 0
            npol = 2 if npass3 else 1
            ring = 4 // npol * npol
            for n in range(nslice):
                ev_prev = cnt[ev_guard] if ev_guard else 0
                for kc in range(nkc):
                    slots = [(idx * npol + p) % 4 for p in range(npol)]

                    @block.sync
                    def _(q):
                        if idx == 0:
                            wat(q, "tt")
                        if idx * npol >= 4:
                            q.wait_ge(sems["wc"], wcb + idx - (4 // npol - 1))
                        for p in range(npol):
                            src = w_srcs[p]
                            sl = WS[:, slots[p] * 1024:slots[p] * 1024 + 512]
                            inc(q.dma_start(
                                out=sl,
                                in_=src[kc * 128:(kc + 1) * 128,
                                        n * 512:(n + 1) * 512]), "qd", 16)
                    qd_w = cnt["qd"]

                    @block.tensor
                    def _(t):
                        wat(t, "qd", qd_w)
                        if kc == 0 and n > 0 and ev_guard:
                            wat(t, ev_guard, ev_prev)
                        last = None
                        for tt in range(2):
                            ps = ps_of(tt, n)
                            wh = WS[:, slots[0] * 1024:slots[0] * 1024 + 512]
                            if npass3:
                                wl = WS[:, slots[1] * 1024:slots[1] * 1024 + 512]
                                sh = stat_fn(kc, tt, 0)
                                sl_ = stat_fn(kc, tt, 1)
                                t.matmul(ps, sh, wh, start=(kc == 0), stop=False)
                                t.matmul(ps, sh, wl, start=False, stop=False)
                                last = t.matmul(ps, sl_, wh, start=False,
                                                stop=False)
                            else:
                                sh = stat_fn(kc, tt, 0)
                                last = t.matmul(ps, sh, wh, start=(kc == 0),
                                                stop=False)
                        inc(last, "wc", 1)
                    idx += 1

                @block.tensor
                def _(t):
                    last = None
                    for tt in range(2):
                        last = t.matmul(ps_of(tt, n),
                                        ROWS[0:1, R_ONES:R_ONES + 128],
                                        bias_row(n), start=False, stop=True)
                    inc(last, "tt", 1)
                tt_e = cnt["tt"]
                for tt in range(2):
                    evict_fn(tt, n, ps_of(tt, n), tt_e)

        pA = MA[:, 0:4096]
        pB = MA[:, 4096:8192]
        pC = MA[:, 8192:12288]
        pD = XNS[:, 0:4096]
        pXNF = MA[:, 12288:16384]
        pre_ps = [psA[:, i * 512:(i + 1) * 512] for i in range(4)] + \
                 [psB[:, i * 512:(i + 1) * 512] for i in range(4)]
        p_xh_tm = CTX[:, 0:1024]
        p_xl_tm = CTX[:, 1024:2048]
        p_xhf = CTX[:, 2048:6144]
        p_xlf = CTX[:, 6144:10240]

        gd_pre = cnt["gd"]
        if DBG:
            @block.gpsimd
            def _(g):
                wat(g, "qd")
                inc(g.dma_start(out=dbg_wfcq, in_=wfcq_s), "gd", 16)
                inc(g.dma_start(out=dbg_wpjq, in_=wpjq_s), "gd", 16)
        gd_pre = cnt["gd"]

        @block.sync
        def _(q):
            wat(q, "gd", gd_pre)

        for pt in range(NPRE):
            barrier()
            for tt in range(4):
                ln1_tile(xp, pt * PTS + tt * 128, STT[:, 0:1024],
                         p_xh_tm, p_xl_tm)
                vv_spl = cnt["vv"]

                @block.sync
                def _(q):
                    wat(q, "vv", vv_spl)
                    for src, dst in [(p_xh_tm, p_xhf), (p_xl_tm, p_xlf)]:
                        for fc in range(NCH):
                            inc(q.dma_start(
                                out=dst[:, fc * PTS + tt * 128:
                                        fc * PTS + (tt + 1) * 128],
                                in_=src[:, fc * 128:(fc + 1) * 128],
                                transpose=True), "qd", 16)
                qd_dm = cnt["qd"]

                @block.vector
                def _(v):
                    wat(v, "qd", qd_dm)
                    last = None
                    for fc in range(NCH):
                        sl = slice(fc * PTS + tt * 128, fc * PTS + (tt + 1) * 128)
                        last = v.tensor_tensor(pXNF[:, sl], p_xhf[:, sl],
                                               p_xlf[:, sl], A.add)
                    inc(last, "vv", 1)
            qd_ps = cnt["qd"] + 16

            @block.sync
            def _(q):
                wat(q, "vv")
                inc(q.dma_start(out=PSI[:, 0:PTS],
                                in_=pspre[:, pt * PTS:(pt + 1) * PTS]), "qd", 16)

            @block.vector
            def _(v):
                wat(v, "qd", qd_ps)
                inc(v.tensor_copy(SCL[:, 514:515], CON[:, 0:1]), "vv", 1)

            omega_phase(p_xhf, p_xlf, PTS, pre_ps, PSI[:, 0:PTS], pA)
            scan_phase(pA, pB, PTS, CPHI)
            sincos_phase(pB, pC, pD)

            @block.vector
            def _(v):
                for c in range(NCH):
                    sl = slice(c * PTS, (c + 1) * PTS)
                    v.scalar_tensor_tensor(pB[:, sl], pXNF[:, sl], 1.0, pD[:, sl],
                                           A.mult, A.mult,
                                           accum_out=CAR[:, ACR + c:ACR + c + 1])
                for c in range(NCH):
                    sl = slice(c * PTS, (c + 1) * PTS)
                    last = v.scalar_tensor_tensor(
                        pB[:, sl], pXNF[:, sl], 1.0, pC[:, sl], A.mult, A.mult,
                        accum_out=CAR[:, ACI + c:ACI + c + 1])
                vf(v, last)
                vf(v, v.tensor_tensor(CAR[:, CR:CR + 8], CAR[:, CR:CR + 8],
                                      CAR[:, ACR:ACR + 8], A.add))
                inc(v.tensor_tensor(CAR[:, CI:CI + 8], CAR[:, CI:CI + 8],
                                    CAR[:, ACI:ACI + 8], A.add), "vv", 1)

        mXNF = MA[:, 0:2048]
        mA = MA[:, 2048:4096]
        mB = MA[:, 4096:6144]
        mC = MA[:, 6144:8192]
        mD = MA[:, 8192:10240]
        mE = MA[:, 10240:12288]
        mG = MA[:, 12288:14336]
        mR2 = MA[:, 14336:16384]
        xn_tm = XNS[:, 0:2048]
        s_tm = XNS[:, 2048:4096]
        g_t = XNS[:, 0:1024]
        fH = MA[:, 0:8192]
        fT1 = MA[:, 8192:10240]
        fT2 = MA[:, 10240:12288]
        fT3 = MA[:, 12288:14336]
        fO = MA[:, 14336:15360]
        h1g = MA[:, 0:4096]
        xh_tm = XSP[:, 0:2048]
        xl_tm = XSP[:, 2048:4096]
        xhf = XSP[:, 4096:6144]
        xlf = XSP[:, 6144:8192]
        chB = CTX[:, 0:8192]
        clB = CTX[:, 8192:16384]
        h1gh_tm = CTX[:, 0:4096]
        h1gl_tm = CTX[:, 4096:8192]
        h1ghf = CTX[:, 8192:12288]
        h1glf = CTX[:, 12288:16384]
        xq_tm = CTX[:, 0:2048]
        xqf = CTX[:, 2048:4096]
        hq_tm = CTX[:, 4096:8192]
        hqf = CTX[:, 8192:16384]
        main_ps = [psA[:, i * 512:i * 512 + 256] for i in range(4)] + \
                  [psB[:, i * 512:i * 512 + 256] for i in range(4)]

        for sub in range(NSUB):
            barrier()
            t0 = sub * TS
            for tt in range(2):
                ln1_tile(xm, t0 + tt * 128, xn_tm[:, tt * 1024:(tt + 1) * 1024],
                         xh_tm[:, tt * 1024:(tt + 1) * 1024],
                         xl_tm[:, tt * 1024:(tt + 1) * 1024])
            vv_spl = cnt["vv"]

            @block.sync
            def _(q):
                wat(q, "vv", vv_spl)
                for src, dst in [(xh_tm, xhf), (xl_tm, xlf)]:
                    for tt in range(2):
                        for fc in range(NCH):
                            inc(q.dma_start(
                                out=dst[:, fc * TS + tt * 128:
                                        fc * TS + (tt + 1) * 128],
                                in_=src[:, tt * 1024 + fc * 128:
                                        tt * 1024 + (fc + 1) * 128],
                                transpose=True), "qd", 16)
                inc(q.dma_start(out=PSI[:, 0:TS],
                                in_=psmain[:, t0:t0 + TS]), "qd", 16)
                inc(q.dma_start(out=PSI[:, 512:512 + TS],
                                in_=ipmain[:, t0:t0 + TS]), "qd", 16)
            qd_dm = cnt["qd"]

            @block.vector
            def _(v):
                wat(v, "qd", qd_dm)
                wat(v, "tt")
                wat(v, "gd")
                last = None
                for fc in range(NCH):
                    sl = slice(fc * TS, (fc + 1) * TS)
                    last = v.tensor_tensor(mXNF[:, sl], xhf[:, sl], xlf[:, sl],
                                           A.add)
                inc(last, "vv", 1)

            barrier()
            omega_phase(xhf, xlf, TS, main_ps, PSI[:, 0:TS], mA)
            barrier()
            scan_phase(mA, mB, TS, CPHI)
            if DBG and sub == 0:
                vv_d = cnt["vv"]

                @block.gpsimd
                def _(g):
                    wat(g, "vv", vv_d)
                    inc(g.dma_start(out=dbg_xn, in_=xn_tm[:]), "gd", 16)
                    inc(g.dma_start(out=dbg_phi, in_=mB[:]), "gd", 16)
            barrier()
            sincos_phase(mB, mC, mD)
            if DBG and sub == 0:
                vv_d2 = cnt["vv"]

                @block.gpsimd
                def _(g):
                    wat(g, "vv", vv_d2)
                    inc(g.dma_start(out=dbg_cos, in_=mD[:]), "gd", 16)

            @block.vector
            def _(v):
                for c in range(NCH):
                    sl = slice(c * TS, (c + 1) * TS)
                    v.tensor_tensor(mE[:, sl], mXNF[:, sl], mD[:, sl], A.mult)
                inc(v.tensor_tensor(blk32(mG), blk32(mXNF), blk32(mC), A.mult)
                    if False else
                    v.tensor_tensor(mG[:], mXNF[:], mC[:], A.mult), "vv", 1)
            barrier()
            if DBG:
                vv_car = cnt["vv"]

                @block.gpsimd
                def _(g):
                    wat(g, "vv", vv_car)
                    inc(g.dma_start(out=dbg_car[:, sub * 48:(sub + 1) * 48],
                                    in_=CAR[:]), "gd", 16)
            scan_phase(mE, mB, TS, CR)
            scan_phase(mG, mA, TS, CI)

            @block.vector
            def _(v):
                ip = PSI[:, 512:512 + TS]
                for c in range(NCH):
                    sl = slice(c * TS, (c + 1) * TS)
                    v.tensor_tensor(mC[:, sl], mC[:, sl], ip, A.mult)
                    v.tensor_tensor(mD[:, sl], mD[:, sl], ip, A.mult)
                v.tensor_tensor(mXNF[:], mB[:], mD[:], A.mult)
                v.tensor_tensor(mR2[:], mA[:], mC[:], A.mult)
                v.tensor_tensor(mXNF[:], mXNF[:], mR2[:], A.add)
                v.tensor_tensor(mR2[:], mA[:], mD[:], A.mult)
                v.tensor_tensor(mA[:], mB[:], mC[:], A.mult)
                inc(v.tensor_tensor(mR2[:], mR2[:], mA[:], A.subtract), "vv", 1)
            barrier()
            vv_ctx = cnt["vv"]
            ctx_parts = [mE, mG, mXNF, mR2]
            if DBG and sub == 0:
                @block.gpsimd
                def _(g):
                    wat(g, "vv", vv_ctx)
                    for i, part in enumerate(ctx_parts):
                        inc(g.dma_start(out=dbg_ctx[:, i * 4096:i * 4096 + 2048],
                                        in_=part[:]), "gd", 16)

            barrier()
            @block.vector
            def _(v):
                wat(v, "vv", vv_ctx)
                last = None
                for i, part in enumerate(ctx_parts):
                    last = v.tensor_copy(chB[:, i * 2048:(i + 1) * 2048], part[:])
                inc(last, "vv", 1)
            vv_craw = cnt["vv"]

            @block.scalar
            def _(s):
                wat(s, "vv", vv_ctx)
                last = None
                for i, part in enumerate(ctx_parts):
                    last = s.activation(clB[:, i * 2048:(i + 1) * 2048], part[:],
                                        AF.Square, bias=0.0, scale=1.0)
                inc(last, "ss", 1)
            ss_csq = cnt["ss"]

            @block.tensor
            def _(t):
                wat(t, "vv", vv_craw)
                wat(t, "ss", ss_csq)
                last = None
                for kc in range(32):
                    t.matmul(psB[:, 0:256], OB[:], chB[:, kc * 256:(kc + 1) * 256],
                             start=(kc == 0), stop=(kc == 31))
                    last = t.matmul(psB[:, 512:768], OB[:],
                                    clB[:, kc * 256:(kc + 1) * 256],
                                    start=(kc == 0), stop=(kc == 31))
                inc(last, "tt", 1)
            tt_st = cnt["tt"]

            @block.vector
            def _(v):
                wat(v, "tt", tt_st)
                v.tensor_scalar(STT[:, 0:256], psB[:, 0:256], 1.0 / 4096.0,
                                None, A.mult)
                vf(v, v.tensor_scalar(STT[:, 256:512], psB[:, 512:768],
                                      1.0 / 4096.0, None, A.mult))
                vf(v, v.tensor_tensor(STT[:, 512:768], STT[:, 0:256],
                                      STT[:, 0:256], A.mult))
                inc(v.tensor_tensor(STT[:, 256:512], STT[:, 256:512],
                                    STT[:, 512:768], A.subtract), "vv", 1)
            vv_var = cnt["vv"]

            @block.scalar
            def _(s):
                wat(s, "vv", vv_var)
                inc(s.activation(STT[:, 256:512], STT[:, 256:512], AF.Sqrt,
                                 bias=CON[:, CEPS:CEPS + 1], scale=1.0), "ss", 1)
            ss_rb = cnt["ss"]

            barrier()
            @block.vector
            def _(v):
                wat(v, "ss", ss_rb)
                vf(v, v.reciprocal(STT[:, 256:512], STT[:, 256:512]))
                last = None
                for part in ctx_parts:
                    p3 = part[:].rearrange("p (c t) -> p c t", t=TS)
                    v.tensor_tensor(p3, p3, bc8(STT[:, 0:256], TS), A.subtract)
                    last = v.tensor_tensor(p3, p3, bc8(STT[:, 256:512], TS),
                                           A.mult)
                inc(last, "vv", 1)
            vv_nrm = cnt["vv"]

            @block.scalar
            def _(s):
                wat(s, "vv", vv_nrm)
                last = None
                for kc in range(32):
                    part = ctx_parts[kc // 8]
                    sl = part[:, (kc % 8) * TS:(kc % 8 + 1) * TS]
                    last = s.activation(sl, sl, AF.Identity,
                                        bias=G2C[:, 32 + kc:32 + kc + 1],
                                        scale=G2C[:, kc:kc + 1])
                inc(last, "ss", 1)
            ss_gb = cnt["ss"]

            @block.vector
            def _(v):
                wat(v, "ss", ss_gb)
                last = None
                for i, part in enumerate(ctx_parts):
                    v.tensor_copy(chB[:, i * 2048:(i + 1) * 2048], part[:])
                    last = v.scalar_tensor_tensor(
                        clB[:, i * 2048:(i + 1) * 2048],
                        chB[:, i * 2048:(i + 1) * 2048], -1.0, part[:],
                        A.mult, A.add)
                inc(last, "vv", 1)
            vv_chcl = cnt["vv"]

            barrier()
            def w1_stat(kc, tt, pol):
                buf = chB if pol == 0 else clB
                return buf[:, kc * 256 + tt * 128:kc * 256 + (tt + 1) * 128]

            @block.tensor
            def _(t):
                wat(t, "vv", vv_chcl)

            def w1_evict(tt, n, ps, tt_e):
                @block.scalar
                def _(s):
                    wat(s, "tt", tt_e)
                    inc(s.activation(
                        h1g[:, tt * 2048 + n * 512:tt * 2048 + (n + 1) * 512],
                        ps, AF.Gelu, bias=0.0, scale=1.0), "ss", 1)

            stream_mm([w1h_s, w1l_s], 32, 2048, w1_stat,
                      lambda tt, n: psA[:, tt * 512:(tt + 1) * 512],
                      lambda n: ROWS[0:1, R_B1 + n * 512:R_B1 + (n + 1) * 512],
                      w1_evict, npass3=True)
            ss_h1 = cnt["ss"]

            @block.vector
            def _(v):
                wat(v, "ss", ss_h1)
                wat(v, "tt")
                v.tensor_copy(h1gh_tm[:], h1g[:])
                inc(v.scalar_tensor_tensor(h1gl_tm[:], h1gh_tm[:], -1.0, h1g[:],
                                           A.mult, A.add), "vv", 1)
            vv_h1s = cnt["vv"]

            @block.sync
            def _(q):
                wat(q, "vv", vv_h1s)
                for src, dst in [(h1gh_tm, h1ghf), (h1gl_tm, h1glf)]:
                    for tt in range(2):
                        for kc in range(16):
                            inc(q.dma_start(
                                out=dst[:, kc * 256 + tt * 128:
                                        kc * 256 + (tt + 1) * 128],
                                in_=src[:, tt * 2048 + kc * 128:
                                        tt * 2048 + (kc + 1) * 128],
                                transpose=True), "qd", 16)
            qd_h1 = cnt["qd"]

            barrier()
            def w2_stat(kc, tt, pol):
                buf = h1ghf if pol == 0 else h1glf
                return buf[:, kc * 256 + tt * 128:kc * 256 + (tt + 1) * 128]

            @block.tensor
            def _(t):
                wat(t, "qd", qd_h1)

            def w2_evict(tt, n, ps, tt_e):
                @block.vector
                def _(v):
                    wat(v, "tt", tt_e)
                    inc(v.scalar_tensor_tensor(
                        s_tm[:, tt * 1024 + n * 512:tt * 1024 + (n + 1) * 512],
                        ps, 1.0,
                        xn_tm[:, tt * 1024 + n * 512:tt * 1024 + (n + 1) * 512],
                        A.mult, A.add), "vv", 1)

            stream_mm([w2h_s, w2l_s], 16, 1024, w2_stat,
                      lambda tt, n: psB[:, (tt * 2 + n) * 512:
                                        (tt * 2 + n + 1) * 512],
                      lambda n: ROWS[0:1, R_B2 + n * 512:R_B2 + (n + 1) * 512],
                      w2_evict, npass3=True, ev_guard=None)

            if DBG and sub == 0:
                vv_s = cnt["vv"]

                @block.gpsimd
                def _(g):
                    wat(g, "vv", vv_s)
                    inc(g.dma_start(out=dbg_s[:, 0:2048], in_=s_tm[:]), "gd", 16)
                    inc(g.dma_start(out=dbg_h1[:], in_=h1g[:]), "gd", 16)
            tt_w2 = cnt["tt"]

            @block.vector
            def _(v):
                wat(v, "tt", tt_w2)
                inc(v.tensor_copy(SCL[:, 516:517], CON[:, 0:1]), "vv", 1)
            for tt in range(2):
                emit_quant(xn_tm[:, tt * 1024:(tt + 1) * 1024],
                           xq_tm[:, tt * 1024:(tt + 1) * 1024], 1024,
                           fT1[:, 0:1024], fT2[:, 0:1024], fT3[:, 0:1024],
                           tie=False)
            vv_xq = cnt["vv"]

            if DBG and sub == 0:
                @block.gpsimd
                def _(g):
                    wat(g, "vv", vv_xq)
                    inc(g.dma_start(out=dbg_xq, in_=xq_tm[:]), "gd", 16)

            @block.sync
            def _(q):
                wat(q, "vv", vv_xq)
                for tt in range(2):
                    for fc in range(NCH):
                        inc(q.dma_start(
                            out=xqf[:, fc * 256 + tt * 128:
                                    fc * 256 + (tt + 1) * 128],
                            in_=xq_tm[:, tt * 1024 + fc * 128:
                                      tt * 1024 + (fc + 1) * 128],
                            transpose=True), "qd", 16)
            qd_xq = cnt["qd"]

            barrier()
            def f1_stat(kc, tt, pol):
                return xqf[:, kc * 256 + tt * 128:kc * 256 + (tt + 1) * 128]

            @block.tensor
            def _(t):
                wat(t, "qd", qd_xq)

            def f1_evict(tt, n, ps, tt_e):
                @block.scalar
                def _(s):
                    wat(s, "tt", tt_e)
                    inc(s.activation(
                        fH[:, tt * 4096 + n * 512:tt * 4096 + (n + 1) * 512],
                        ps, AF.Gelu, bias=0.0, scale=1.0), "ss", 1)

            stream_mm([wfcq_s], 8, 4096, f1_stat,
                      lambda tt, n: psA[:, tt * 512:(tt + 1) * 512],
                      lambda n: ROWS[0:1, R_BFC + n * 512:R_BFC + (n + 1) * 512],
                      f1_evict, npass3=False, ev_guard="ss")
            ss_ffh = cnt["ss"]

            barrier()
            @block.vector
            def _(v):
                wat(v, "ss", ss_ffh)
                inc(v.tensor_copy(SCL[:, 517:518], CON[:, 0:1]), "vv", 1)
            for tt in range(2):
                if tt == 1:
                    qd_prev = cnt["qd"]

                    @block.vector
                    def _(v):
                        wat(v, "qd", qd_prev)
                        inc(v.tensor_copy(SCL[:, 518:519], CON[:, 0:1]), "vv", 1)
                for hh in range(2):
                    emit_quant(fH[:, tt * 4096 + hh * 2048:
                                  tt * 4096 + (hh + 1) * 2048],
                               hq_tm[:, hh * 2048:(hh + 1) * 2048],
                               2048, fT1[:], fT2[:], fT3[:], tie=False)
                vv_hq = cnt["vv"]
                if DBG and sub == 0 and tt == 0:
                    @block.gpsimd
                    def _(g):
                        wat(g, "vv", vv_hq)
                        inc(g.dma_start(out=dbg_hq, in_=hq_tm[:]), "gd", 16)

                @block.sync
                def _(q):
                    wat(q, "vv", vv_hq)
                    for kc in range(32):
                        inc(q.dma_start(
                            out=hqf[:, tt * 4096 + kc * 128:
                                    tt * 4096 + (kc + 1) * 128],
                            in_=hq_tm[:, kc * 128:(kc + 1) * 128],
                            transpose=True), "qd", 16)
            qd_hq = cnt["qd"]

            barrier()
            def f2_stat(kc, tt, pol):
                return hqf[:, tt * 4096 + kc * 128:tt * 4096 + (kc + 1) * 128]

            @block.tensor
            def _(t):
                wat(t, "qd", qd_hq)

            g_tm = XNS[:, 0:2048]

            def f2_evict(tt, n, ps, tt_e):
                @block.scalar
                def _(s):
                    wat(s, "tt", tt_e)
                    inc(s.activation(
                        g_tm[:, tt * 1024 + n * 512:tt * 1024 + (n + 1) * 512],
                        ps, AF.Silu, bias=0.0, scale=1.0), "ss", 1)

            stream_mm([wpjq_s], 32, 1024, f2_stat,
                      lambda tt, n: psB[:, (tt * 2 + n) * 512:
                                        (tt * 2 + n + 1) * 512],
                      lambda n: ROWS[0:1, R_BPJ + n * 512:R_BPJ + (n + 1) * 512],
                      f2_evict, npass3=False, ev_guard=None)
            ss_g = cnt["ss"]

            if DBG and sub == 0:
                ss_gd = cnt["ss"]

                @block.gpsimd
                def _(g):
                    wat(g, "ss", ss_gd)
                    inc(g.dma_start(out=dbg_g, in_=g_tm[:]), "gd", 16)
            for tt in range(2):
                qd_x = cnt["qd"] + 16

                @block.sync
                def _(q):
                    wat(q, "vv")
                    inc(q.dma_start(out=XT[:],
                                    in_=xm[t0 + tt * 128:t0 + (tt + 1) * 128, :]),
                        "qd", 16)

                @block.vector
                def _(v):
                    wat(v, "qd", qd_x)
                    wat(v, "ss", ss_g)
                    wat(v, "gd")
                    v.scalar_tensor_tensor(fO[:], s_tm[:, tt * 1024:
                                                       (tt + 1) * 1024],
                                           1.0, g_tm[:, tt * 1024:
                                                     (tt + 1) * 1024],
                                           A.mult, A.mult)
                    inc(v.tensor_tensor(fO[:], fO[:], XT[:], A.add), "vv", 1)
                vv_o = cnt["vv"]

                @block.gpsimd
                def _(g):
                    wat(g, "vv", vv_o)
                    inc(g.dma_start(out=y[t0 + tt * 128:t0 + (tt + 1) * 128, :],
                                    in_=fO[:]), "gd", 16)
                    if DBG and sub == 0:
                        inc(g.dma_start(out=dbg_g2[:, tt * 1024:(tt + 1) * 1024],
                                        in_=g_tm[:, tt * 1024:(tt + 1) * 1024]),
                            "gd", 16)
                        inc(g.dma_start(out=dbg_s2[:, tt * 1024:(tt + 1) * 1024],
                                        in_=s_tm[:, tt * 1024:(tt + 1) * 1024]),
                            "gd", 16)

        @block.gpsimd
        def _(g):
            wat(g, "gd")
    return nc


_NC_CACHE = []


def kernel(**inputs):
    x = np.ascontiguousarray(inputs["L0_attn_x"], dtype=np.float32)
    B, S, _ = x.shape
    w = {
        "wom": np.ascontiguousarray(inputs["L0_attn_W_omega"], np.float32),
        "bomf": np.ascontiguousarray(
            np.asarray(inputs["L0_attn_b_omega"], np.float32).reshape(8, 128).T),
        "lscf": np.ascontiguousarray(
            np.asarray(inputs["L0_attn_log_scale"], np.float32).reshape(8, 128).T),
        "g2f": np.ascontiguousarray(
            np.asarray(inputs["L0_attn_ln_gamma"], np.float32).reshape(32, 128).T),
        "b2gf": np.ascontiguousarray(
            np.asarray(inputs["L0_attn_ln_beta"], np.float32).reshape(32, 128).T),
        "w1": np.ascontiguousarray(inputs["L0_attn_W1"], np.float32),
        "b1r": np.asarray(inputs["L0_attn_b1"], np.float32).reshape(1, -1),
        "w2": np.ascontiguousarray(inputs["L0_attn_W2"], np.float32),
        "b2r": np.asarray(inputs["L0_attn_b2"], np.float32).reshape(1, -1),
        "wfc": np.ascontiguousarray(inputs["L0_ffn_w_fc"], np.float32),
        "bfcr": np.asarray(inputs["L0_ffn_b_fc"], np.float32).reshape(1, -1),
        "wpj": np.ascontiguousarray(inputs["L0_ffn_w_proj"], np.float32),
        "bpjr": np.asarray(inputs["L0_ffn_b_proj"], np.float32).reshape(1, -1),
    }
    in_maps = []
    for c in range(8):
        b, j = c // 4, c % 4
        s0 = j * CH
        xmv = np.ascontiguousarray(x[b, s0:s0 + CH])
        xpv = np.zeros((PRE, D), np.float32)
        psp = np.zeros(PRE, np.float32)
        nv = min(s0, PRE)
        if nv:
            xpv[PRE - nv:] = x[b, s0 - nv:s0]
            gpos = np.arange(s0 - nv + 1, s0 + 1, dtype=np.float32)
            psp[PRE - nv:] = 1.0 / np.sqrt(gpos)
        pos = np.arange(s0 + 1, s0 + CH + 1, dtype=np.float32)
        m = dict(w)
        m["xm"] = xmv
        m["xp"] = xpv
        m["pspre"] = np.ascontiguousarray(np.tile(psp[None, :], (128, 1)))
        m["psmain"] = np.ascontiguousarray(
            np.tile((1.0 / np.sqrt(pos))[None, :], (128, 1)))
        m["ipmain"] = np.ascontiguousarray(np.tile((1.0 / pos)[None, :], (128, 1)))
        in_maps.append(m)
    if not _NC_CACHE:
        _NC_CACHE.append(build_nc())
    nc = _NC_CACHE[0]
    res = run_bass_kernel_spmd(nc, in_maps, core_ids=list(range(8)))
    out = np.empty((B, S, D), np.float32)
    for c in range(8):
        b, j = c // 4, c % 4
        out[b, j * CH:(j + 1) * CH] = res.results[c]["y"]
    return out
